# version 14
# baseline (speedup 1.0000x reference)
import numpy as np
import ml_dtypes
import concourse.bass as bass
import concourse.mybir as mybir
from concourse.bass_utils import run_bass_kernel_spmd
from contextlib import ExitStack

F32 = mybir.dt.float32
BF16 = mybir.dt.bfloat16
U16 = mybir.dt.uint16
AF = mybir.ActivationFunctionType
ALU = mybir.AluOpType
AX = mybir.AxisListType

B, L, D = 4, 2048, 2048
H = 16
E = 2048
NTOK = 1024
N = 16384
NSG = 16
SGN = 1024
KBT = 8
GC1 = 0.07135481672260025
GC2 = 1.5957691216057308
INV_D = 1.0 / 2048.0
EPS = 1e-5


class Ctr:
    __slots__ = ("v",)

    def __init__(self):
        self.v = 0


def build():
    nc = bass.Bass("TRN2", target_bir_lowering=False)

    x_ext = nc.dram_tensor("x_ext", [E, D], F32, kind="ExternalInput")
    igw = nc.dram_tensor("igw", [H, 128, 128], F32, kind="ExternalInput")
    agw = nc.dram_tensor("agw", [H, 128, 128], F32, kind="ExternalInput")
    gbT = nc.dram_tensor("gbT", [128, H], F32, kind="ExternalInput")
    abT = nc.dram_tensor("abT", [128, H], F32, kind="ExternalInput")
    apT = nc.dram_tensor("apT", [128, H], F32, kind="ExternalInput")
    cfA = nc.dram_tensor("cfA", [128, 2], F32, kind="ExternalInput")
    cfB = nc.dram_tensor("cfB", [128, 2], F32, kind="ExternalInput")
    idf = nc.dram_tensor("idf", [128, 128], F32, kind="ExternalInput")
    idb = nc.dram_tensor("idb", [128, 128], BF16, kind="ExternalInput")
    cst = nc.dram_tensor("cst", [128, 8], F32, kind="ExternalInput")
    pkg = nc.dram_tensor("pkg", [KBT, 128, N], U16, kind="ExternalInput")
    pku = nc.dram_tensor("pku", [KBT, 128, N], U16, kind="ExternalInput")
    pkd = nc.dram_tensor("pkd", [64, 128, D], U16, kind="ExternalInput")
    sgT = nc.dram_tensor("sgT", [64, N], F32, kind="ExternalInput")
    suT = nc.dram_tensor("suT", [64, N], F32, kind="ExternalInput")
    sdT = nc.dram_tensor("sdT", [512, D], F32, kind="ExternalInput")
    y_out = nc.dram_tensor("y_out", [NTOK, D], F32, kind="ExternalOutput")
    import os
    DBG = os.environ.get("KDBG") == "1"
    if DBG:
        dbg_dec = nc.dram_tensor("dbg_dec", [128, 16 * 1024], BF16, kind="ExternalOutput")
        dbg_wd = nc.dram_tensor("dbg_wd", [128, 8 * 2048], BF16, kind="ExternalOutput")
        dbg_int = nc.dram_tensor("dbg_int", [128, 8 * 1024], BF16, kind="ExternalOutput")
        dbg_i0 = nc.dram_tensor("dbg_i0", [128, 8 * 1024], BF16, kind="ExternalOutput")
        dbg_xne = nc.dram_tensor("dbg_xne", [128, 8192], BF16, kind="ExternalOutput")
        dbg_oac = nc.dram_tensor("dbg_oac", [H, 128, NTOK], F32, kind="ExternalOutput")

    xn_dram = nc.dram_tensor("xn_dram", [16, 128, D], F32, kind="Internal")
    y_dram = nc.dram_tensor("y_dram", [H, 128, NTOK], F32, kind="Internal")
    oacc_dram = nc.dram_tensor("oacc_dram", [H, 128, NTOK], F32, kind="Internal")
    sgbf_d = nc.dram_tensor("sgbf_d", [64, N], BF16, kind="Internal")
    subf_d = nc.dram_tensor("subf_d", [64, N], BF16, kind="Internal")
    sdbf_d = nc.dram_tensor("sdbf_d", [512, D], BF16, kind="Internal")

    with ExitStack() as g:
        eg = g.enter_context
        xnE = eg(nc.sbuf_tensor("xnE", [128, 8192], BF16))
        xnO = eg(nc.sbuf_tensor("xnO", [128, 8192], BF16))
        cstt = eg(nc.sbuf_tensor("cstt", [128, 8], F32))
        idft = eg(nc.sbuf_tensor("idft", [128, 128], F32))
        idbt = eg(nc.sbuf_tensor("idbt", [128, 128], BF16))
        gbt = eg(nc.sbuf_tensor("gbt", [128, H], F32))
        abt = eg(nc.sbuf_tensor("abt", [128, H], F32))
        apt = eg(nc.sbuf_tensor("apt", [128, H], F32))
        cfa = eg(nc.sbuf_tensor("cfa", [128, 2], F32))
        cfb = eg(nc.sbuf_tensor("cfb", [128, 2], F32))
        msp8 = eg(nc.sbuf_tensor("msp8", [128, H], F32))
        spt = eg(nc.sbuf_tensor("spt", [128, H * 4], F32))

        nc.const_aps.register(F32, -2.0, cstt[:, 0:1])
        nc.const_aps.register(F32, -6.0, cstt[:, 1:2])
        nc.const_aps.register(F32, EPS, cstt[:, 2:3])
        nc.const_aps.register(F32, -1.0, cstt[:, 3:4])
        nc.const_aps.register(F32, 1.0, cstt[:, 4:5])

        sems = {}
        ctrs = {}
        for name in ["s_cst", "s_scl", "s_x0", "s_x1", "s_xn0", "s_xn1", "s_xh",
                     "s_y", "s_pk0", "s_pk1", "s_sc0", "s_sc1", "s_acc0", "s_acc1",
                     "s_ld0", "s_ld1", "s_out", "s_l5", "dv", "da", "dt"]:
            sems[name] = eg(nc.semaphore(name))
            ctrs[name] = Ctr()
        block = eg(nc.Block())

        def inc(instr, name, n=1):
            instr.then_inc(sems[name], n)
            ctrs[name].v += n

        def wait(eng, name, val=None):
            eng.wait_ge(sems[name], ctrs[name].v if val is None else val)

        def dma4(eng, out_ap, in_ap, sem_name, n=4):
            F = out_ap.shape[-1]
            w = F // n
            for q in range(n):
                inc(eng.dma_start(out=out_ap[:, q * w:(q + 1) * w],
                                  in_=in_ap[:, q * w:(q + 1) * w]), sem_name, 16)

        with ExitStack() as p012:
            ep = p012.enter_context
            igw_sb = ep(nc.sbuf_tensor("igw_sb", [128, H * 128], F32))
            agw_sb = ep(nc.sbuf_tensor("agw_sb", [128, H * 128], F32))

            @block.sync
            def _(sync):
                for h in range(H):
                    inc(sync.dma_start(out=igw_sb[:, 128 * h:128 * (h + 1)], in_=igw.ap()[h]), "s_cst", 16)
                    inc(sync.dma_start(out=agw_sb[:, 128 * h:128 * (h + 1)], in_=agw.ap()[h]), "s_cst", 16)
                for dst, src in ((gbt, gbT), (abt, abT), (apt, apT), (cfa, cfA),
                                 (cfb, cfB), (idft, idf), (idbt, idb), (cstt, cst)):
                    inc(sync.dma_start(out=dst[:], in_=src.ap()), "s_cst", 16)
            cst_done = ctrs["s_cst"].v

            @block.scalar
            def _(scalar):
                wait(scalar, "s_cst", cst_done)
                scalar.activation(spt[:, 0:H], apt[:], AF.Abs)
                inc(scalar.activation(spt[:, 3 * H:4 * H], apt[:], AF.Relu), "da")
                wait(scalar, "da")
                inc(scalar.activation(spt[:, H:2 * H], spt[:, 0:H], AF.Exp, scale=-1.0), "da")
                wait(scalar, "da")
                inc(scalar.activation(spt[:, 2 * H:3 * H], spt[:, H:2 * H], AF.Ln, bias=1.0), "da")

            @block.vector
            def _(vector):
                wait(vector, "s_cst", cst_done)
                wait(vector, "da")
                inc(vector.tensor_tensor(msp8[:], spt[:, 3 * H:4 * H], spt[:, 2 * H:3 * H], ALU.add), "dv")
                wait(vector, "dv")
                inc(vector.tensor_scalar(msp8[:], msp8[:], -8.0, None, ALU.mult), "dv")

            with ExitStack() as prep:
                epp = prep.enter_context
                sf32 = epp(nc.sbuf_tensor("sf32", [64, N], F32))
                sbf = epp(nc.sbuf_tensor("sbf", [64, N], BF16))

                prep_steps = ([(sgT, sgbf_d, N, 0)] + [(suT, subf_d, N, 0)] +
                              [(sdT, sdbf_d, D, 64 * k) for k in range(8)])
                for src, dst, cols, roff in prep_steps:

                    @block.sync
                    def _(sync, src=src, cols=cols, roff=roff):
                        wait(sync, "s_scl")
                        dma4(sync, sf32[:, 0:cols], src.ap()[roff:roff + 64, :], "s_scl")

                    @block.vector
                    def _(vector, cols=cols):
                        wait(vector, "s_scl")
                        nq = 4
                        w = cols // nq
                        for q2 in range(nq):
                            ins = vector.tensor_copy(sbf[:, q2 * w:(q2 + 1) * w],
                                                     sf32[:, q2 * w:(q2 + 1) * w])
                        inc(ins, "dv")

                    @block.sync
                    def _(sync, dst=dst, cols=cols, roff=roff):
                        wait(sync, "dv")
                        inc(sync.dma_start(out=dst.ap()[roff:roff + 64, :],
                                           in_=sbf[:, 0:cols]), "s_scl", 16)
                compact_done = ctrs["s_scl"].v
                scl_done = compact_done

            x_t = [ep(nc.sbuf_tensor(f"x_t{p}", [128, D], F32)) for p in range(2)]
            xn_t = [ep(nc.sbuf_tensor(f"xn_t{p}", [128, D], F32)) for p in range(2)]
            sq_scr = [ep(nc.sbuf_tensor(f"sq_scr{p}", [128, D], F32)) for p in range(2)]
            stat = ep(nc.sbuf_tensor("stat", [128, 16 * 6], F32))
            xe_tok = ep(nc.sbuf_tensor("xe_tok", [128, 1024], BF16))
            xo_tok = ep(nc.sbuf_tensor("xo_tok", [128, 1024], BF16))
            psbf = ep(nc.psum_tensor("psbf", [128, 512], BF16))
            psf = [ep(nc.psum_tensor(f"psf{i}", [128, 512], F32)) for i in range(5)]

            dv_xn_done = [0] * 16
            da_sq_done = [0] * 16

            for i in range(16):
                p = i % 2

                @block.sync
                def _(sync, i=i, p=p):
                    if i == 0:
                        wait(sync, "s_scl", compact_done)
                    if i >= 2:
                        wait(sync, "dv", dv_xn_done[i - 2])
                        wait(sync, "da", da_sq_done[i - 2])
                    dma4(sync, x_t[p][:], x_ext.ap()[128 * i:128 * (i + 1), :], f"s_x{i % 2}")

                @block.vector
                def _(vector, i=i, p=p):
                    if i == 0:
                        wait(vector, "s_cst", cst_done)
                    wait(vector, f"s_x{i % 2}")
                    inc(vector.reduce_sum(stat[:, 6 * i:6 * i + 1], x_t[p][:], axis=AX.X), "dv")

                @block.scalar
                def _(scalar, i=i, p=p):
                    wait(scalar, f"s_x{i % 2}")
                    inc(scalar.activation(sq_scr[p][:], x_t[p][:], AF.Square,
                                          accum_out=stat[:, 6 * i + 1:6 * i + 2]), "da")
                    da_sq_done[i] = ctrs["da"].v

                @block.vector
                def _(vector, i=i):
                    wait(vector, "dv")
                    wait(vector, "da")
                    c0 = 6 * i
                    inc(vector.tensor_scalar(stat[:, c0 + 2:c0 + 3], stat[:, c0:c0 + 1], INV_D, None, ALU.mult), "dv")
                    wait(vector, "dv")
                    vector.tensor_scalar(stat[:, c0 + 3:c0 + 4], stat[:, c0 + 1:c0 + 2], INV_D, None, ALU.mult)
                    inc(vector.tensor_single_scalar(stat[:, c0:c0 + 1], stat[:, c0 + 2:c0 + 3],
                                                    stat[:, c0 + 2:c0 + 3], ALU.mult), "dv")
                    wait(vector, "dv")
                    inc(vector.tensor_tensor(stat[:, c0 + 3:c0 + 4], stat[:, c0 + 3:c0 + 4],
                                             stat[:, c0:c0 + 1], ALU.subtract), "dv")

                @block.scalar
                def _(scalar, i=i):
                    wait(scalar, "dv")
                    c0 = 6 * i
                    inc(scalar.activation(stat[:, c0 + 4:c0 + 5], stat[:, c0 + 3:c0 + 4],
                                          AF.Sqrt, bias=EPS), "da")

                @block.vector
                def _(vector, i=i, p=p):
                    wait(vector, "da")
                    c0 = 6 * i
                    inc(vector.reciprocal(stat[:, c0 + 5:c0 + 6], stat[:, c0 + 4:c0 + 5]), "dv")
                    wait(vector, "dv")
                    if i >= 2:
                        wait(vector, f"s_xn{i % 2}")
                    inc(vector.tensor_scalar(xn_t[p][:], x_t[p][:], stat[:, c0 + 2:c0 + 3],
                                             stat[:, c0 + 5:c0 + 6], ALU.subtract, ALU.mult), "dv")
                    dv_xn_done[i] = ctrs["dv"].v

                @block.sync
                def _(sync, i=i, p=p):
                    wait(sync, "dv")
                    dma4(sync, xn_dram.ap()[i], xn_t[p][:], f"s_xn{i % 2}")

                if i >= 8:
                    o = i - 8

                    @block.vector
                    def _(vector, p=p):
                        wait(vector, "dv")
                        wait(vector, "dt")
                        vector.tensor_copy(xe_tok[:], xn_t[p][:, 0:D:2])
                        inc(vector.tensor_copy(xo_tok[:], xn_t[p][:, 1:D:2]), "dv")

                    for tok, dstt in ((xe_tok, xnE), (xo_tok, xnO)):
                        for j4 in range(2):

                            @block.tensor
                            def _(tensor, tok=tok, j4=j4, i=i):
                                if i == 8 and tok is xe_tok and j4 == 0:
                                    wait(tensor, "s_cst", cst_done)
                                wait(tensor, "dv")
                                for jj in range(4):
                                    ins = tensor.transpose(psbf[:, 128 * jj:128 * (jj + 1)],
                                                           tok[:, 128 * (4 * j4 + jj):128 * (4 * j4 + jj + 1)],
                                                           idbt[:])
                                inc(ins, "dt")

                            @block.vector
                            def _(vector, dstt=dstt, o=o, j4=j4):
                                wait(vector, "dt")
                                for jj in range(4):
                                    j = 4 * j4 + jj
                                    ins = vector.tensor_copy(
                                        dstt[:, 1024 * j + 128 * o:1024 * j + 128 * (o + 1)],
                                        psbf[:, 128 * jj:128 * (jj + 1)])
                                inc(ins, "dv")

            xh_tok = ep(nc.sbuf_tensor("xh_tok", [128, E], F32))
            xn_h = ep(nc.sbuf_tensor("xn_h", [128, E], F32))
            sgx = ep(nc.sbuf_tensor("sgx", [128, E], F32))
            sga = ep(nc.sbuf_tensor("sga", [128, E], F32))
            loga = ep(nc.sbuf_tensor("loga", [128, E], F32))
            a_t = ep(nc.sbuf_tensor("a_t", [128, E], F32))
            asq = ep(nc.sbuf_tensor("asq", [128, E], F32))
            mlt = ep(nc.sbuf_tensor("mlt", [128, E], F32))
            nx1 = ep(nc.sbuf_tensor("nx1", [128, E], F32))
            nx = ep(nc.sbuf_tensor("nx", [128, E], F32))
            ysc = ep(nc.sbuf_tensor("ysc", [128, E], F32))

            for h in range(H):

                @block.sync
                def _(sync, h=h):
                    wait(sync, "s_xn0")
                    wait(sync, "s_xn1")
                    wait(sync, "dv")
                    wait(sync, "dt")
                    for i in range(16):
                        inc(sync.dma_start(out=xh_tok[:, 128 * i:128 * (i + 1)],
                                           in_=xn_dram.ap()[i][:, 128 * h:128 * (h + 1)]), "s_xh", 16)

                for i4 in range(4):

                    @block.tensor
                    def _(tensor, i4=i4):
                        wait(tensor, "s_xh")
                        if i4 >= 1:
                            wait(tensor, "da")
                        for ii in range(4):
                            i = 4 * i4 + ii
                            ins = tensor.transpose(psf[4][:, 128 * ii:128 * (ii + 1)],
                                                   xh_tok[:, 128 * i:128 * (i + 1)], idft[:])
                        inc(ins, "dt")

                    @block.scalar
                    def _(scalar, i4=i4):
                        wait(scalar, "dt")
                        inc(scalar.activation(xn_h[:, 512 * i4:512 * (i4 + 1)], psf[4][:], AF.Copy), "da")

                for wsb, bsb, dst in ((igw_sb, gbt, sgx), (agw_sb, abt, sga)):

                    @block.tensor
                    def _(tensor, h=h, wsb=wsb):
                        wait(tensor, "da")
                        for c4 in range(4):
                            ins = tensor.matmul(psf[c4][:], wsb[:, 128 * h:128 * (h + 1)],
                                                xn_h[:, 512 * c4:512 * (c4 + 1)],
                                                start=True, stop=True)
                        inc(ins, "dt")

                    @block.scalar
                    def _(scalar, h=h, bsb=bsb, dst=dst):
                        wait(scalar, "dt")
                        for c4 in range(4):
                            ins = scalar.activation(dst[:, 512 * c4:512 * (c4 + 1)], psf[c4][:],
                                                    AF.Sigmoid, bias=bsb[:, h:h + 1])
                        inc(ins, "da")

                @block.vector
                def _(vector, h=h):
                    wait(vector, "da")
                    inc(vector.tensor_single_scalar(loga[:], sga[:], msp8[:, h:h + 1], ALU.mult), "dv")

                @block.scalar
                def _(scalar):
                    wait(scalar, "dv")
                    inc(scalar.activation(a_t[:], loga[:], AF.Exp), "da")
                    wait(scalar, "da")
                    inc(scalar.activation(asq[:], a_t[:], AF.Square), "da")
                    wait(scalar, "da")
                    inc(scalar.activation(mlt[:], asq[:], AF.Sqrt, bias=1.0, scale=-1.0), "da")

                @block.vector
                def _(vector, h=h):
                    wait(vector, "da")
                    vector.tensor_scalar(mlt[:, 0:1], mlt[:, 0:1], cfa[:, 0:1], cfb[:, 0:1], ALU.mult, ALU.add)
                    inc(vector.tensor_scalar(mlt[:, 1024:1025], mlt[:, 1024:1025],
                                             cfa[:, 1:2], cfb[:, 1:2], ALU.mult, ALU.add), "dv")
                    inc(vector.tensor_tensor(nx1[:], xn_h[:], sgx[:], ALU.mult), "dv")
                    wait(vector, "dv")
                    inc(vector.tensor_tensor(nx[:], nx1[:], mlt[:], ALU.mult), "dv")
                    wait(vector, "dv")
                    if h >= 1:
                        wait(vector, "s_y")
                    inc(vector.tensor_tensor_scan(ysc[:], a_t[:], nx[:], 0.0, ALU.mult, ALU.add), "dv")

                @block.sync
                def _(sync, h=h):
                    wait(sync, "dv")
                    dma4(sync, y_dram.ap()[h], ysc[:, NTOK:E], "s_y", n=2)

            p2_dv = ctrs["dv"].v
            p2_da = ctrs["da"].v
            p2_sy = ctrs["s_y"].v

        with ExitStack() as p4:
            e4 = p4.enter_context
            dec_g = e4(nc.sbuf_tensor("dec_g", [128, 16 * 1024], BF16))
            dec_u = e4(nc.sbuf_tensor("dec_u", [128, 16 * 1024], BF16))
            wd_dec = e4(nc.sbuf_tensor("wd_dec", [128, 8 * 2048], BF16))
            inter = e4(nc.sbuf_tensor("inter", [128, 8 * 1024], BF16))
            i0b = e4(nc.sbuf_tensor("i0b", [128, 8 * 1024], BF16))
            pk_t = [e4(nc.sbuf_tensor(f"pk_t{p}", [128, 1024], U16)) for p in range(2)]
            sc_t = [e4(nc.sbuf_tensor(f"sc_t{p}", [128, 1024], BF16)) for p in range(2)]
            base_b = [[e4(nc.sbuf_tensor(f"base_{r}_{k}", [128, 1024], U16)) for k in range(2)] for r in range(2)]
            t1_b = [e4(nc.sbuf_tensor(f"t1_{r}", [128, 1024], U16)) for r in range(2)]
            tb_b = [e4(nc.sbuf_tensor(f"tb_{k}", [128, 1024], BF16)) for k in range(2)]
            mag_b = [e4(nc.sbuf_tensor(f"mag_{k}", [128, 1024], BF16)) for k in range(2)]
            gf_st = e4(nc.sbuf_tensor("gf_st", [128, 1024], F32))
            q2_st = e4(nc.sbuf_tensor("q2_st", [128, 1024], F32))
            sm_st = e4(nc.sbuf_tensor("sm_st", [128, 1024], F32))
            ub_st = sm_st[:].bitcast(BF16)[:, 0:1024]
            pb = [e4(nc.psum_tensor(f"pb{i}", [128, 512], F32)) for i in range(8)]

            SH2 = {"lo": 6, "hi": 2}
            SH1 = {"lo": 12, "hi": 8}
            load_free = {}
            lctr = [0]
            uctr = [0]
            prev = [None]
            ph2_g1 = {}
            ph2_g2 = {}
            ph2_g3 = {}

            class U:
                __slots__ = ("uidx", "lidx", "slot", "nibs", "dsts", "par",
                             "war_dt", "spk", "ssc", "ph1_dv", "act_da")

            def emit_unit(pk_src, sc_src, nibs, dsts, war_dt):
                u = U()
                u.uidx = uctr[0]; uctr[0] += 1
                u.lidx = lctr[0]; lctr[0] += 1
                u.slot = u.lidx % 2
                u.par = u.uidx % 2
                u.nibs, u.dsts, u.war_dt = nibs, dsts, war_dt
                pkn = f"s_pk{u.slot}"
                scn = f"s_sc{u.slot}"

                @block.sync
                def _(sync):
                    if u.uidx == 0:
                        wait(sync, "s_scl", scl_done)
                        wait(sync, "s_y", p2_sy)
                        wait(sync, "dv", p2_dv)
                        wait(sync, "da", p2_da)
                        wait(sync, "s_xn0")
                        wait(sync, "s_xn1")
                    if u.lidx >= 2:
                        wait(sync, "dv", load_free[u.lidx - 2])
                    inc(sync.dma_start(out=pk_t[u.slot][:], in_=pk_src), pkn, 16)
                    inc(sync.dma_start(out=sc_t[u.slot][:], in_=sc_src), scn, 16)
                u.spk = ctrs[pkn].v
                u.ssc = ctrs[scn].v

                @block.vector
                def _(vector):
                    wait(vector, pkn, u.spk)
                    pk = pk_t[u.slot][:]
                    nk = len(u.nibs)
                    for k, nib in enumerate(u.nibs):
                        vector.tensor_scalar(base_b[u.par][k][:], pk, SH2[nib], 0x1C0,
                                             ALU.logical_shift_left, ALU.bitwise_and)
                    ins = vector.tensor_scalar(t1_b[u.par][:], pk, SH1[u.nibs[0]], 0x8000,
                                               ALU.logical_shift_left, ALU.bitwise_and)
                    inc(ins, "dv")
                    wait(vector, "dv")
                    if nk == 2:
                        vector.tensor_scalar(pk, pk, SH1[u.nibs[1]], 0x8000,
                                             ALU.logical_shift_left, ALU.bitwise_and)
                    for k in range(nk):
                        ins = vector.tensor_scalar(base_b[u.par][k][:], base_b[u.par][k][:],
                                                   0x3F00, None, ALU.add)
                    inc(ins, "dv")
                u.ph1_dv = ctrs["dv"].v

                if prev[0] is not None:
                    emit_ph2(prev[0])
                    prev[0] = None
                prev[0] = u

            def emit_ph2(u):
                @block.vector
                def _(vector):
                    wait(vector, "dv", u.ph1_dv)
                    nk = len(u.nibs)
                    t1s = [t1_b[u.par][:], pk_t[u.slot][:]]
                    for k in range(nk):
                        ins = vector.tensor_scalar(tb_b[k][:], base_b[u.par][k][:].bitcast(BF16),
                                                   2.0, -1.0, ALU.mult, ALU.add)
                    inc(ins, "dv")
                    wait(vector, "dv")
                    for k in range(nk):
                        ins = vector.tensor_tensor(mag_b[k][:], tb_b[k][:],
                                                   base_b[u.par][k][:].bitcast(BF16), ALU.min)
                    inc(ins, "dv")
                    ph2_g1[u.uidx] = ctrs["dv"].v
                    wait(vector, "dv")
                    for k in range(nk):
                        ins = vector.tensor_tensor(base_b[u.par][k][:], mag_b[k][:].bitcast(U16),
                                                   t1s[k], ALU.bitwise_or)
                    inc(ins, "dv")
                    ph2_g2[u.uidx] = ctrs["dv"].v
                    wait(vector, "dv")
                    wait(vector, f"s_sc{u.slot}", u.ssc)
                    if u.war_dt is not None:
                        wait(vector, "dt", u.war_dt)
                    for k in range(nk):
                        ins = vector.tensor_tensor(u.dsts[k], base_b[u.par][k][:].bitcast(BF16),
                                                   sc_t[u.slot][:], ALU.mult)
                    inc(ins, "dv")
                    ph2_g3[u.uidx] = ctrs["dv"].v
                load_free[u.lidx] = ctrs["dv"].v

            def flush():
                if prev[0] is not None:
                    emit_ph2(prev[0])
                    prev[0] = None
                return ctrs["dv"].v

            gelu_rrsa_dv = {}
            gelu_i0_dv = {}
            gelu_bank_dv = {}
            ub_bank_dv = {}

            def emit_gfq2(pg, gate_dt):
                ntl = pg % 4

                @block.vector
                def _(vector):
                    wait(vector, "dt", gate_dt)
                    vector.tensor_copy(gf_st[:, 0:512], pb[2 * ntl][:])
                    ins = vector.tensor_copy(gf_st[:, 512:1024], pb[2 * ntl + 1][:])
                    inc(ins, "dv")
                    gelu_bank_dv[pg] = ctrs["dv"].v
                    wait(vector, "dv")
                    vector.tensor_tensor(q2_st[:, 0:512], gf_st[:, 0:512],
                                         gf_st[:, 0:512], ALU.mult)
                    ins = vector.tensor_tensor(q2_st[:, 512:1024], gf_st[:, 512:1024],
                                               gf_st[:, 512:1024], ALU.mult)
                    inc(ins, "dv")
                    wait(vector, "dv")
                    inc(vector.tensor_scalar(q2_st[:], q2_st[:], GC1, GC2, ALU.mult, ALU.add), "dv")
                    wait(vector, "dv")
                    inc(vector.tensor_tensor(sm_st[:], q2_st[:], gf_st[:], ALU.mult), "dv")
                gelu_rrsa_dv[pg] = ctrs["dv"].v
                return gelu_rrsa_dv[pg]

            def emit_sig(pg):
                @block.scalar
                def _(scalar):
                    wait(scalar, "dv", gelu_rrsa_dv[pg])
                    inc(scalar.activation(q2_st[:], sm_st[:], AF.Sigmoid), "da")
                return ctrs["da"].v

            def emit_i0(pg, sig_da, grp):
                nt = 4 * grp + pg % 4

                @block.vector
                def _(vector):
                    wait(vector, "da", sig_da)
                    wait(vector, "s_acc0")
                    wait(vector, "s_acc1")
                    inc(vector.tensor_tensor(
                        i0b[:, 1024 * nt:1024 * (nt + 1)],
                        q2_st[:], gf_st[:], ALU.mult), "dv")
                gelu_i0_dv[pg] = ctrs["dv"].v

            PENDING = []

            def drain(n=None):
                k = len(PENDING) if n is None else min(n, len(PENDING))
                for _ in range(k):
                    PENDING.pop(0)()

            evac_dv = {}
            dq_holder = {}
            down_dt = {}

            for sg in range(NSG):
                lo_region = sg < 8
                m = sg % 8
                war_g = gate_war_dt if sg > 0 else None
                war_u = up_war_dt if sg > 0 else None

                for kbt in range(KBT):
                    emit_unit(pkg.ap()[kbt][:, SGN * sg:SGN * (sg + 1)],
                              sgbf_d.ap()[8 * kbt:8 * (kbt + 1), SGN * sg:SGN * (sg + 1)]
                              .unsqueeze(1).broadcast_to((8, 16, SGN)),
                              ["lo", "hi"],
                              [dec_g[:, 1024 * (2 * kbt):1024 * (2 * kbt + 1)],
                               dec_g[:, 1024 * (2 * kbt + 1):1024 * (2 * kbt + 2)]], war_g)
                    drain(1)
                g_done = flush()
                drain()

                @block.tensor
                def _(tensor, g_done=g_done):
                    wait(tensor, "dv", g_done)
                    for step in range(16):
                        kbt, nib = step // 2, step % 2
                        xsrc = xnE if nib == 0 else xnO
                        for ntl in range(4):
                            for tc in range(2):
                                ins = tensor.matmul(
                                    pb[2 * ntl + tc][:],
                                    dec_g[:, 1024 * step + 128 * ntl:1024 * step + 128 * (ntl + 1)],
                                    xsrc[:, 1024 * kbt + 512 * tc:1024 * kbt + 512 * (tc + 1)],
                                    start=(step == 0), stop=(step == 15))
                    inc(ins, "dt")
                gate0_dt = ctrs["dt"].v

                def queue_gelu(grp, gate_dt):
                    for j in range(4):
                        pg = 4 * grp + j

                        def piece(pg=pg, grp=grp, gate_dt=gate_dt):
                            emit_gfq2(pg, gate_dt)
                            sd_ = emit_sig(pg)
                            emit_i0(pg, sd_, grp)
                        PENDING.append(piece)

                queue_gelu(0, gate0_dt)

                for kbt in range(4):
                    emit_unit(pku.ap()[kbt][:, SGN * sg:SGN * (sg + 1)],
                              subf_d.ap()[8 * kbt:8 * (kbt + 1), SGN * sg:SGN * (sg + 1)]
                              .unsqueeze(1).broadcast_to((8, 16, SGN)),
                              ["lo", "hi"],
                              [dec_u[:, 1024 * (2 * kbt):1024 * (2 * kbt + 1)],
                               dec_u[:, 1024 * (2 * kbt + 1):1024 * (2 * kbt + 2)]], war_u)
                    drain(2)
                drain()

                @block.tensor
                def _(tensor):
                    wait(tensor, "dv", gelu_bank_dv[3])
                    for step in range(16):
                        kbt, nib = step // 2, step % 2
                        xsrc = xnE if nib == 0 else xnO
                        for ntl in range(4):
                            nt = 4 + ntl
                            for tc in range(2):
                                ins = tensor.matmul(
                                    pb[2 * ntl + tc][:],
                                    dec_g[:, 1024 * step + 128 * nt:1024 * step + 128 * (nt + 1)],
                                    xsrc[:, 1024 * kbt + 512 * tc:1024 * kbt + 512 * (tc + 1)],
                                    start=(step == 0), stop=(step == 15))
                    inc(ins, "dt")
                gate1_dt = ctrs["dt"].v
                gate_war_dt = gate1_dt

                queue_gelu(1, gate1_dt)

                for kbt in range(4, KBT):
                    emit_unit(pku.ap()[kbt][:, SGN * sg:SGN * (sg + 1)],
                              subf_d.ap()[8 * kbt:8 * (kbt + 1), SGN * sg:SGN * (sg + 1)]
                              .unsqueeze(1).broadcast_to((8, 16, SGN)),
                              ["lo", "hi"],
                              [dec_u[:, 1024 * (2 * kbt):1024 * (2 * kbt + 1)],
                               dec_u[:, 1024 * (2 * kbt + 1):1024 * (2 * kbt + 2)]], war_u)
                    drain(2)
                u_done = flush()
                drain()

                @block.tensor
                def _(tensor, u_done=u_done):
                    wait(tensor, "dv", max(u_done, gelu_bank_dv[7]))
                    for step in range(16):
                        kbt, nib = step // 2, step % 2
                        xsrc = xnE if nib == 0 else xnO
                        for ntl in range(4):
                            for tc in range(2):
                                ins = tensor.matmul(
                                    pb[2 * ntl + tc][:],
                                    dec_u[:, 1024 * step + 128 * ntl:1024 * step + 128 * (ntl + 1)],
                                    xsrc[:, 1024 * kbt + 512 * tc:1024 * kbt + 512 * (tc + 1)],
                                    start=(step == 0), stop=(step == 15))
                    inc(ins, "dt")
                up0_dt = ctrs["dt"].v

                ubinter_dv = {}

                def queue_ubinter(grp, up_dt):
                    for j in range(4):
                        pj = 4 * grp + j
                        nt = 4 * grp + j

                        def piece(pj=pj, ntl=j, nt=nt, up_dt=up_dt):
                            @block.vector
                            def _(vector):
                                wait(vector, "dt", up_dt)
                                vector.tensor_copy(sm_st[:].bitcast(BF16)[:, 0:512],
                                                   pb[2 * ntl][:])
                                ins = vector.tensor_copy(sm_st[:].bitcast(BF16)[:, 512:1024],
                                                         pb[2 * ntl + 1][:])
                                inc(ins, "dv")
                                ub_bank_dv[pj] = ctrs["dv"].v
                                wait(vector, "dv")
                                inc(vector.tensor_tensor(
                                    inter[:, 1024 * nt:1024 * (nt + 1)],
                                    i0b[:, 1024 * nt:1024 * (nt + 1)],
                                    ub_st, ALU.mult), "dv")
                            ubinter_dv[pj] = ctrs["dv"].v
                        PENDING.append(piece)

                def wd_units(j):
                    nbt = 8 * m + j
                    nib = "lo" if lo_region else "hi"
                    war_d = down_dt.get(sg - 1)
                    for halfload in range(2):
                        base = 2048 * j + 1024 * halfload
                        emit_unit(pkd.ap()[nbt][:, 1024 * halfload:1024 * (halfload + 1)],
                                  sdbf_d.ap()[8 * nbt:8 * (nbt + 1), 1024 * halfload:1024 * (halfload + 1)]
                                  .unsqueeze(1).broadcast_to((8, 16, 1024)),
                                  [nib], [wd_dec[:, base:base + 1024]], war_d)

                queue_ubinter(0, up0_dt)
                for j in range(4):
                    wd_units(j)
                    drain(2)
                drain()

                @block.tensor
                def _(tensor):
                    wait(tensor, "dv", ub_bank_dv[3])
                    for step in range(16):
                        kbt, nib = step // 2, step % 2
                        xsrc = xnE if nib == 0 else xnO
                        for ntl in range(4):
                            nt = 4 + ntl
                            for tc in range(2):
                                ins = tensor.matmul(
                                    pb[2 * ntl + tc][:],
                                    dec_u[:, 1024 * step + 128 * nt:1024 * step + 128 * (nt + 1)],
                                    xsrc[:, 1024 * kbt + 512 * tc:1024 * kbt + 512 * (tc + 1)],
                                    start=(step == 0), stop=(step == 15))
                    inc(ins, "dt")
                up1_dt = ctrs["dt"].v
                up_war_dt = up1_dt

                queue_ubinter(1, up1_dt)
                for j in range(4, 8):
                    wd_units(j)
                    drain(2)
                wd_done = flush()
                drain()
                inter_done = ctrs["dv"].v

                st_acc_q = {}
                for quad in range(4):

                    def quad_mm(quad=quad, sg=sg, wd_done=wd_done, inter_done=inter_done):
                        @block.tensor
                        def _(tensor):
                            wait(tensor, "dv", max(wd_done, inter_done))
                            if quad > 0:
                                wait(tensor, "dv", evac_dv[(sg, quad - 1)])
                            for j in range(8):
                                for htl in range(4):
                                    ht = 4 * quad + htl
                                    for tc in range(2):
                                        ins = tensor.matmul(
                                            pb[2 * htl + tc][:],
                                            wd_dec[:, 2048 * j + 128 * ht:2048 * j + 128 * (ht + 1)],
                                            inter[:, 1024 * j + 512 * tc:1024 * j + 512 * (tc + 1)],
                                            start=(j == 0), stop=(j == 7))
                            inc(ins, "dt")
                        dq_holder[(sg, quad)] = ctrs["dt"].v
                        if quad == 3:
                            down_dt[sg] = ctrs["dt"].v
                    PENDING.append(quad_mm)

                    def quad_evacs(quad=quad, sg=sg, st_acc_q=st_acc_q):
                        dq = dq_holder[(sg, quad)]
                        i0f = i0b[:].bitcast(F32)
                        add_dv = {}
                        for htl in range(4):
                            ht = 4 * quad + htl
                            Ls = i0f[:, 1024 * (htl % 2):1024 * (htl % 2) + 1024]
                            Os = i0f[:, 2048 + 1024 * (htl % 2):2048 + 1024 * (htl % 2) + 1024]
                            ldn = f"s_ld{htl % 2}"
                            acn = f"s_acc{htl % 2}"

                            @block.sync
                            def _(sync, ht=ht, sg=sg, htl=htl, Ls=Ls, ldn=ldn):
                                if sg > 0:
                                    if htl >= 2:
                                        wait(sync, "dv", add_dv[htl - 2])
                                    dma4(sync, Ls, oacc_dram.ap()[ht], ldn, n=2)
                            ld_ = ctrs[ldn].v

                            @block.vector
                            def _(vector, htl=htl, dq=dq, ld_=ld_, sg=sg, quad=quad,
                                  Ls=Ls, Os=Os, ldn=ldn, acn=acn):
                                wait(vector, "dt", dq)
                                if htl >= 2:
                                    wait(vector, acn, st_acc_q[(quad, htl - 2)])
                                elif quad > 0:
                                    wait(vector, acn, st_acc_q[(quad - 1, htl + 2)])
                                if sg > 0:
                                    wait(vector, ldn, ld_)
                                    vector.tensor_tensor(Os[:, 0:512], Ls[:, 0:512],
                                                         pb[2 * htl + 0][:], ALU.add)
                                    inc(vector.tensor_tensor(Os[:, 512:1024], Ls[:, 512:1024],
                                                             pb[2 * htl + 1][:], ALU.add), "dv")
                                else:
                                    vector.tensor_copy(Os[:, 0:512], pb[2 * htl + 0][:])
                                    inc(vector.tensor_copy(Os[:, 512:1024], pb[2 * htl + 1][:]), "dv")
                            add_dv[htl] = ctrs["dv"].v

                            @block.sync
                            def _(sync, ht=ht, htl=htl, Os=Os, acn=acn):
                                wait(sync, "dv", add_dv[htl])
                                dma4(sync, oacc_dram.ap()[ht], Os, acn, n=2)
                            st_acc_q[(quad, htl)] = ctrs[acn].v
                        evac_dv[(sg, quad)] = ctrs["dv"].v
                    PENDING.append(quad_evacs)

            drain()

            p4_acc0 = ctrs["s_acc0"].v
            p4_acc1 = ctrs["s_acc1"].v

            if DBG:
                @block.sync
                def _(sync):
                    wait(sync, "dv")
                    wait(sync, "s_acc0", p4_acc0)
                    wait(sync, "s_acc1", p4_acc1)
                    inc(sync.dma_start(out=dbg_dec.ap(), in_=dec_g[:]), "s_l5", 16)
                    inc(sync.dma_start(out=dbg_wd.ap(), in_=wd_dec[:]), "s_l5", 16)
                    inc(sync.dma_start(out=dbg_int.ap(), in_=inter[:]), "s_l5", 16)
                    inc(sync.dma_start(out=dbg_i0.ap(), in_=i0b[:]), "s_l5", 16)
                    inc(sync.dma_start(out=dbg_xne.ap(), in_=xnE[:]), "s_l5", 16)
                    for ht in range(H):
                        inc(sync.dma_start(out=dbg_oac.ap()[ht], in_=oacc_dram.ap()[ht]), "s_l5", 16)
                    wait(sync, "s_l5")

        with ExitStack() as p5:
            e5 = p5.enter_context
            tot = e5(nc.sbuf_tensor("tot", [128, H * 1024], F32))
            acc_t = e5(nc.sbuf_tensor("acc_t", [128, 1024], F32))
            yh_t = e5(nc.sbuf_tensor("yh_t", [128, 1024], F32))
            xo_t = e5(nc.sbuf_tensor("xo_t", [128, D], F32))
            ot_t = e5(nc.sbuf_tensor("ot_t", [128, D], F32))
            res_t = e5(nc.sbuf_tensor("res_t", [128, D], F32))
            ps5 = e5(nc.psum_tensor("ps5", [128, 512], F32))

            for ht in range(H):

                @block.sync
                def _(sync, ht=ht):
                    if ht == 0:
                        wait(sync, "s_acc0", p4_acc0)
                        wait(sync, "s_acc1", p4_acc1)
                        wait(sync, "s_y", 16 * 2 * H)
                        wait(sync, "dv")
                        wait(sync, "da")
                    else:
                        wait(sync, "dv")
                    dma4(sync, acc_t[:], oacc_dram.ap()[ht], "s_l5", n=2)
                    dma4(sync, yh_t[:], y_dram.ap()[ht], "s_l5", n=2)

                @block.vector
                def _(vector, ht=ht):
                    wait(vector, "s_l5")
                    inc(vector.tensor_tensor(tot[:, 1024 * ht:1024 * (ht + 1)], acc_t[:], yh_t[:], ALU.add), "dv")

            for tt_ in range(8):

                @block.sync
                def _(sync, tt_=tt_):
                    wait(sync, "dv")
                    if tt_ >= 1:
                        wait(sync, "s_out")
                    dma4(sync, xo_t[:], x_ext.ap()[NTOK + 128 * tt_:NTOK + 128 * (tt_ + 1), :], "s_l5")

                for h4 in range(4):

                    @block.tensor
                    def _(tensor, tt_=tt_, h4=h4):
                        wait(tensor, "dv")
                        if not (tt_ == 0 and h4 == 0):
                            wait(tensor, "da")
                        for hh in range(4):
                            hx = 4 * h4 + hh
                            ins = tensor.transpose(ps5[:, 128 * hh:128 * (hh + 1)],
                                                   tot[:, 1024 * hx + 128 * tt_:1024 * hx + 128 * (tt_ + 1)],
                                                   idft[:])
                        inc(ins, "dt")

                    @block.scalar
                    def _(scalar, h4=h4):
                        wait(scalar, "dt")
                        inc(scalar.activation(ot_t[:, 512 * h4:512 * (h4 + 1)], ps5[:], AF.Copy), "da")

                @block.vector
                def _(vector):
                    wait(vector, "da")
                    wait(vector, "s_l5")
                    inc(vector.tensor_tensor(res_t[:], ot_t[:], xo_t[:], ALU.add), "dv")

                @block.sync
                def _(sync, tt_=tt_):
                    wait(sync, "dv")
                    dma4(sync, y_out.ap()[128 * tt_:128 * (tt_ + 1), :], res_t[:], "s_out")

            @block.sync
            def _(sync):
                wait(sync, "s_out")

    return nc


def kernel(L0_attn_x, L0_attn_a_param, L0_attn_ig_w, L0_attn_ig_b,
           L0_attn_ag_w, L0_attn_ag_b, L0_ffn_gate_W_packed, L0_ffn_gate_scales,
           L0_ffn_up_W_packed, L0_ffn_up_scales, L0_ffn_down_W_packed,
           L0_ffn_down_scales):
    X = np.ascontiguousarray(np.asarray(L0_attn_x, dtype=np.float32))

    perm = np.concatenate([np.arange(0, N, 2), np.arange(1, N, 2)])

    def pack_u16(W_packed):
        w = np.ascontiguousarray(np.asarray(W_packed, dtype=np.int32))
        return w.view(np.uint16)[:, 0::2]

    pkg_h = np.ascontiguousarray(pack_u16(L0_ffn_gate_W_packed)[perm].T).reshape(KBT, 128, N)
    pku_h = np.ascontiguousarray(pack_u16(L0_ffn_up_W_packed)[perm].T).reshape(KBT, 128, N)
    pkd_h = np.ascontiguousarray(pack_u16(L0_ffn_down_W_packed).T).reshape(64, 128, D)
    sgT_h = np.ascontiguousarray(np.asarray(L0_ffn_gate_scales, dtype=np.float32)[perm].T)
    suT_h = np.ascontiguousarray(np.asarray(L0_ffn_up_scales, dtype=np.float32)[perm].T)
    sdT_h = np.ascontiguousarray(np.asarray(L0_ffn_down_scales, dtype=np.float32).T)

    igw_h = np.ascontiguousarray(np.asarray(L0_attn_ig_w, dtype=np.float32))
    agw_h = np.ascontiguousarray(np.asarray(L0_attn_ag_w, dtype=np.float32))
    gbT_h = np.ascontiguousarray(np.asarray(L0_attn_ig_b, dtype=np.float32).T)
    abT_h = np.ascontiguousarray(np.asarray(L0_attn_ag_b, dtype=np.float32).T)
    apT_h = np.ascontiguousarray(np.asarray(L0_attn_a_param, dtype=np.float32).reshape(H, 128).T)
    idf_h = np.eye(128, dtype=np.float32)
    idb_h = np.eye(128).astype(ml_dtypes.bfloat16)
    cst_h = np.tile(np.array([[-2.0, -6.0, EPS, -1.0, 1.0, 0.0, 0.0, 0.0]],
                             dtype=np.float32), (128, 1))

    in_maps = []
    for c in range(8):
        b, th = c // 2, c % 2
        own = X[b, NTOK * th:NTOK * (th + 1)]
        pre = X[b, 0:NTOK] if th == 1 else np.zeros((NTOK, D), dtype=np.float32)
        x_ext_h = np.ascontiguousarray(np.concatenate([pre, own], axis=0))
        if th == 1:
            cfA_h = np.concatenate([np.zeros((128, 1)), np.ones((128, 1))], axis=1).astype(np.float32)
            cfB_h = np.concatenate([np.ones((128, 1)), np.zeros((128, 1))], axis=1).astype(np.float32)
        else:
            cfA_h = np.concatenate([np.ones((128, 1)), np.zeros((128, 1))], axis=1).astype(np.float32)
            cfB_h = np.concatenate([np.zeros((128, 1)), np.ones((128, 1))], axis=1).astype(np.float32)
        in_maps.append(dict(
            x_ext=x_ext_h, igw=igw_h, agw=agw_h, gbT=gbT_h, abT=abT_h,
            apT=apT_h, cfA=cfA_h, cfB=cfB_h, idf=idf_h, idb=idb_h, cst=cst_h,
            pkg=pkg_h, pku=pku_h, pkd=pkd_h, sgT=sgT_h, suT=suT_h, sdT=sdT_h))

    nc = build()
    res = run_bass_kernel_spmd(nc, in_maps, core_ids=list(range(8)))

    out = np.empty((B, L, D), dtype=np.float32)
    for c in range(8):
        b, th = c // 2, c % 2
        out[b, NTOK * th:NTOK * (th + 1)] = res.results[c]["y_out"]
    return out


# version 15
# speedup vs baseline: 1.2084x; 1.2084x over previous
import numpy as np
import ml_dtypes
import concourse.bass as bass
import concourse.mybir as mybir
from concourse.bass_utils import run_bass_kernel_spmd
from contextlib import ExitStack

F32 = mybir.dt.float32
BF16 = mybir.dt.bfloat16
U16 = mybir.dt.uint16
AF = mybir.ActivationFunctionType
ALU = mybir.AluOpType
AX = mybir.AxisListType

B, L, D = 4, 2048, 2048
H = 16
E = 2048
NTOK = 1024
N = 16384
NSG = 16
SGN = 1024
KBT = 8
GC1 = 0.07135481672260025
GC2 = 1.5957691216057308
INV_D = 1.0 / 2048.0
EPS = 1e-5


class Ctr:
    __slots__ = ("v",)

    def __init__(self):
        self.v = 0


def build():
    nc = bass.Bass("TRN2", target_bir_lowering=False)

    x_ext = nc.dram_tensor("x_ext", [E, D], F32, kind="ExternalInput")
    igw = nc.dram_tensor("igw", [H, 128, 128], F32, kind="ExternalInput")
    agw = nc.dram_tensor("agw", [H, 128, 128], F32, kind="ExternalInput")
    gbT = nc.dram_tensor("gbT", [128, H], F32, kind="ExternalInput")
    abT = nc.dram_tensor("abT", [128, H], F32, kind="ExternalInput")
    apT = nc.dram_tensor("apT", [128, H], F32, kind="ExternalInput")
    cfA = nc.dram_tensor("cfA", [128, 2], F32, kind="ExternalInput")
    cfB = nc.dram_tensor("cfB", [128, 2], F32, kind="ExternalInput")
    idf = nc.dram_tensor("idf", [128, 128], F32, kind="ExternalInput")
    idb = nc.dram_tensor("idb", [128, 128], BF16, kind="ExternalInput")
    cst = nc.dram_tensor("cst", [128, 8], F32, kind="ExternalInput")
    pkg = nc.dram_tensor("pkg", [KBT, 128, N], U16, kind="ExternalInput")
    pku = nc.dram_tensor("pku", [KBT, 128, N], U16, kind="ExternalInput")
    pkd = nc.dram_tensor("pkd", [64, 128, D], U16, kind="ExternalInput")
    sgT = nc.dram_tensor("sgT", [64, N], F32, kind="ExternalInput")
    suT = nc.dram_tensor("suT", [64, N], F32, kind="ExternalInput")
    sdT = nc.dram_tensor("sdT", [512, D], F32, kind="ExternalInput")
    y_out = nc.dram_tensor("y_out", [NTOK, D], F32, kind="ExternalOutput")
    import os
    DBG = os.environ.get("KDBG") == "1"
    if DBG:
        dbg_dec = nc.dram_tensor("dbg_dec", [128, 16 * 1024], BF16, kind="ExternalOutput")
        dbg_wd = nc.dram_tensor("dbg_wd", [128, 8 * 2048], BF16, kind="ExternalOutput")
        dbg_int = nc.dram_tensor("dbg_int", [128, 8 * 1024], BF16, kind="ExternalOutput")
        dbg_i0 = nc.dram_tensor("dbg_i0", [128, 8 * 1024], BF16, kind="ExternalOutput")
        dbg_xne = nc.dram_tensor("dbg_xne", [128, 8192], BF16, kind="ExternalOutput")
        dbg_oac = nc.dram_tensor("dbg_oac", [H, 128, NTOK], F32, kind="ExternalOutput")

    xn_dram = nc.dram_tensor("xn_dram", [16, 128, D], F32, kind="Internal")
    y_dram = nc.dram_tensor("y_dram", [H, 128, NTOK], F32, kind="Internal")
    oacc_dram = nc.dram_tensor("oacc_dram", [H, 128, NTOK], F32, kind="Internal")
    sgbf_d = nc.dram_tensor("sgbf_d", [64, N], BF16, kind="Internal")
    subf_d = nc.dram_tensor("subf_d", [64, N], BF16, kind="Internal")
    sdbf_d = nc.dram_tensor("sdbf_d", [512, D], BF16, kind="Internal")

    with ExitStack() as g:
        eg = g.enter_context
        xnE = eg(nc.sbuf_tensor("xnE", [128, 8192], BF16))
        xnO = eg(nc.sbuf_tensor("xnO", [128, 8192], BF16))
        cstt = eg(nc.sbuf_tensor("cstt", [128, 8], F32))
        idft = eg(nc.sbuf_tensor("idft", [128, 128], F32))
        idbt = eg(nc.sbuf_tensor("idbt", [128, 128], BF16))
        gbt = eg(nc.sbuf_tensor("gbt", [128, H], F32))
        abt = eg(nc.sbuf_tensor("abt", [128, H], F32))
        apt = eg(nc.sbuf_tensor("apt", [128, H], F32))
        cfa = eg(nc.sbuf_tensor("cfa", [128, 2], F32))
        cfb = eg(nc.sbuf_tensor("cfb", [128, 2], F32))
        msp8 = eg(nc.sbuf_tensor("msp8", [128, H], F32))
        spt = eg(nc.sbuf_tensor("spt", [128, H * 4], F32))

        nc.const_aps.register(F32, -2.0, cstt[:, 0:1])
        nc.const_aps.register(F32, -6.0, cstt[:, 1:2])
        nc.const_aps.register(F32, EPS, cstt[:, 2:3])
        nc.const_aps.register(F32, -1.0, cstt[:, 3:4])
        nc.const_aps.register(F32, 1.0, cstt[:, 4:5])

        sems = {}
        ctrs = {}
        for name in ["s_cst", "s_scl", "s_x0", "s_x1", "s_xn0", "s_xn1", "s_xh",
                     "s_y", "s_pk0", "s_pk1", "s_sc0", "s_sc1", "s_acc0", "s_acc1",
                     "s_ld0", "s_ld1", "s_out", "s_l5", "dv", "da", "dt", "ev"]:
            sems[name] = eg(nc.semaphore(name))
            ctrs[name] = Ctr()
        block = eg(nc.Block())

        def inc(instr, name, n=1):
            instr.then_inc(sems[name], n)
            ctrs[name].v += n

        def wait(eng, name, val=None):
            eng.wait_ge(sems[name], ctrs[name].v if val is None else val)

        def dma4(eng, out_ap, in_ap, sem_name, n=4):
            F = out_ap.shape[-1]
            w = F // n
            for q in range(n):
                inc(eng.dma_start(out=out_ap[:, q * w:(q + 1) * w],
                                  in_=in_ap[:, q * w:(q + 1) * w]), sem_name, 16)

        with ExitStack() as p012:
            ep = p012.enter_context
            igw_sb = ep(nc.sbuf_tensor("igw_sb", [128, H * 128], F32))
            agw_sb = ep(nc.sbuf_tensor("agw_sb", [128, H * 128], F32))

            @block.sync
            def _(sync):
                for h in range(H):
                    inc(sync.dma_start(out=igw_sb[:, 128 * h:128 * (h + 1)], in_=igw.ap()[h]), "s_cst", 16)
                    inc(sync.dma_start(out=agw_sb[:, 128 * h:128 * (h + 1)], in_=agw.ap()[h]), "s_cst", 16)
                for dst, src in ((gbt, gbT), (abt, abT), (apt, apT), (cfa, cfA),
                                 (cfb, cfB), (idft, idf), (idbt, idb), (cstt, cst)):
                    inc(sync.dma_start(out=dst[:], in_=src.ap()), "s_cst", 16)
            cst_done = ctrs["s_cst"].v

            @block.scalar
            def _(scalar):
                wait(scalar, "s_cst", cst_done)
                scalar.activation(spt[:, 0:H], apt[:], AF.Abs)
                inc(scalar.activation(spt[:, 3 * H:4 * H], apt[:], AF.Relu), "da")
                wait(scalar, "da")
                inc(scalar.activation(spt[:, H:2 * H], spt[:, 0:H], AF.Exp, scale=-1.0), "da")
                wait(scalar, "da")
                inc(scalar.activation(spt[:, 2 * H:3 * H], spt[:, H:2 * H], AF.Ln, bias=1.0), "da")

            @block.vector
            def _(vector):
                wait(vector, "s_cst", cst_done)
                wait(vector, "da")
                inc(vector.tensor_tensor(msp8[:], spt[:, 3 * H:4 * H], spt[:, 2 * H:3 * H], ALU.add), "dv")
                wait(vector, "dv")
                inc(vector.tensor_scalar(msp8[:], msp8[:], -8.0, None, ALU.mult), "dv")

            with ExitStack() as prep:
                epp = prep.enter_context
                sf32 = epp(nc.sbuf_tensor("sf32", [64, N], F32))
                sbf = epp(nc.sbuf_tensor("sbf", [64, N], BF16))

                prep_steps = ([(sgT, sgbf_d, N, 0)] + [(suT, subf_d, N, 0)] +
                              [(sdT, sdbf_d, D, 64 * k) for k in range(8)])
                for src, dst, cols, roff in prep_steps:

                    @block.sync
                    def _(sync, src=src, cols=cols, roff=roff):
                        wait(sync, "s_scl")
                        dma4(sync, sf32[:, 0:cols], src.ap()[roff:roff + 64, :], "s_scl")

                    @block.vector
                    def _(vector, cols=cols):
                        wait(vector, "s_scl")
                        nq = 4
                        w = cols // nq
                        for q2 in range(nq):
                            ins = vector.tensor_copy(sbf[:, q2 * w:(q2 + 1) * w],
                                                     sf32[:, q2 * w:(q2 + 1) * w])
                        inc(ins, "dv")

                    @block.sync
                    def _(sync, dst=dst, cols=cols, roff=roff):
                        wait(sync, "dv")
                        inc(sync.dma_start(out=dst.ap()[roff:roff + 64, :],
                                           in_=sbf[:, 0:cols]), "s_scl", 16)
                compact_done = ctrs["s_scl"].v
                scl_done = compact_done

            x_t = [ep(nc.sbuf_tensor(f"x_t{p}", [128, D], F32)) for p in range(2)]
            xn_t = [ep(nc.sbuf_tensor(f"xn_t{p}", [128, D], F32)) for p in range(2)]
            sq_scr = [ep(nc.sbuf_tensor(f"sq_scr{p}", [128, D], F32)) for p in range(2)]
            stat = ep(nc.sbuf_tensor("stat", [128, 16 * 6], F32))
            xe_tok = ep(nc.sbuf_tensor("xe_tok", [128, 1024], BF16))
            xo_tok = ep(nc.sbuf_tensor("xo_tok", [128, 1024], BF16))
            psbf = ep(nc.psum_tensor("psbf", [128, 512], BF16))
            psf = [ep(nc.psum_tensor(f"psf{i}", [128, 512], F32)) for i in range(5)]

            dv_xn_done = [0] * 16
            da_sq_done = [0] * 16

            for i in range(16):
                p = i % 2

                @block.sync
                def _(sync, i=i, p=p):
                    if i == 0:
                        wait(sync, "s_scl", compact_done)
                    if i >= 2:
                        wait(sync, "dv", dv_xn_done[i - 2])
                        wait(sync, "da", da_sq_done[i - 2])
                    dma4(sync, x_t[p][:], x_ext.ap()[128 * i:128 * (i + 1), :], f"s_x{i % 2}")

                @block.vector
                def _(vector, i=i, p=p):
                    if i == 0:
                        wait(vector, "s_cst", cst_done)
                    wait(vector, f"s_x{i % 2}")
                    inc(vector.reduce_sum(stat[:, 6 * i:6 * i + 1], x_t[p][:], axis=AX.X), "dv")

                @block.scalar
                def _(scalar, i=i, p=p):
                    wait(scalar, f"s_x{i % 2}")
                    inc(scalar.activation(sq_scr[p][:], x_t[p][:], AF.Square,
                                          accum_out=stat[:, 6 * i + 1:6 * i + 2]), "da")
                    da_sq_done[i] = ctrs["da"].v

                @block.vector
                def _(vector, i=i):
                    wait(vector, "dv")
                    wait(vector, "da")
                    c0 = 6 * i
                    inc(vector.tensor_scalar(stat[:, c0 + 2:c0 + 3], stat[:, c0:c0 + 1], INV_D, None, ALU.mult), "dv")
                    wait(vector, "dv")
                    vector.tensor_scalar(stat[:, c0 + 3:c0 + 4], stat[:, c0 + 1:c0 + 2], INV_D, None, ALU.mult)
                    inc(vector.tensor_single_scalar(stat[:, c0:c0 + 1], stat[:, c0 + 2:c0 + 3],
                                                    stat[:, c0 + 2:c0 + 3], ALU.mult), "dv")
                    wait(vector, "dv")
                    inc(vector.tensor_tensor(stat[:, c0 + 3:c0 + 4], stat[:, c0 + 3:c0 + 4],
                                             stat[:, c0:c0 + 1], ALU.subtract), "dv")

                @block.scalar
                def _(scalar, i=i):
                    wait(scalar, "dv")
                    c0 = 6 * i
                    inc(scalar.activation(stat[:, c0 + 4:c0 + 5], stat[:, c0 + 3:c0 + 4],
                                          AF.Sqrt, bias=EPS), "da")

                @block.vector
                def _(vector, i=i, p=p):
                    wait(vector, "da")
                    c0 = 6 * i
                    inc(vector.reciprocal(stat[:, c0 + 5:c0 + 6], stat[:, c0 + 4:c0 + 5]), "dv")
                    wait(vector, "dv")
                    if i >= 2:
                        wait(vector, f"s_xn{i % 2}")
                    inc(vector.tensor_scalar(xn_t[p][:], x_t[p][:], stat[:, c0 + 2:c0 + 3],
                                             stat[:, c0 + 5:c0 + 6], ALU.subtract, ALU.mult), "dv")
                    dv_xn_done[i] = ctrs["dv"].v

                @block.sync
                def _(sync, i=i, p=p):
                    wait(sync, "dv")
                    dma4(sync, xn_dram.ap()[i], xn_t[p][:], f"s_xn{i % 2}")

                if i >= 8:
                    o = i - 8

                    @block.vector
                    def _(vector, p=p):
                        wait(vector, "dv")
                        wait(vector, "dt")
                        vector.tensor_copy(xe_tok[:], xn_t[p][:, 0:D:2])
                        inc(vector.tensor_copy(xo_tok[:], xn_t[p][:, 1:D:2]), "dv")

                    for tok, dstt in ((xe_tok, xnE), (xo_tok, xnO)):
                        for j4 in range(2):

                            @block.tensor
                            def _(tensor, tok=tok, j4=j4, i=i):
                                if i == 8 and tok is xe_tok and j4 == 0:
                                    wait(tensor, "s_cst", cst_done)
                                wait(tensor, "dv")
                                for jj in range(4):
                                    ins = tensor.transpose(psbf[:, 128 * jj:128 * (jj + 1)],
                                                           tok[:, 128 * (4 * j4 + jj):128 * (4 * j4 + jj + 1)],
                                                           idbt[:])
                                inc(ins, "dt")

                            @block.vector
                            def _(vector, dstt=dstt, o=o, j4=j4):
                                wait(vector, "dt")
                                for jj in range(4):
                                    j = 4 * j4 + jj
                                    ins = vector.tensor_copy(
                                        dstt[:, 1024 * j + 128 * o:1024 * j + 128 * (o + 1)],
                                        psbf[:, 128 * jj:128 * (jj + 1)])
                                inc(ins, "dv")

            xh_tok = ep(nc.sbuf_tensor("xh_tok", [128, E], F32))
            xn_h = ep(nc.sbuf_tensor("xn_h", [128, E], F32))
            sgx = ep(nc.sbuf_tensor("sgx", [128, E], F32))
            sga = ep(nc.sbuf_tensor("sga", [128, E], F32))
            loga = ep(nc.sbuf_tensor("loga", [128, E], F32))
            a_t = ep(nc.sbuf_tensor("a_t", [128, E], F32))
            asq = ep(nc.sbuf_tensor("asq", [128, E], F32))
            mlt = ep(nc.sbuf_tensor("mlt", [128, E], F32))
            nx1 = ep(nc.sbuf_tensor("nx1", [128, E], F32))
            nx = ep(nc.sbuf_tensor("nx", [128, E], F32))
            ysc = ep(nc.sbuf_tensor("ysc", [128, E], F32))

            for h in range(H):

                @block.sync
                def _(sync, h=h):
                    wait(sync, "s_xn0")
                    wait(sync, "s_xn1")
                    wait(sync, "dv")
                    wait(sync, "dt")
                    for i in range(16):
                        inc(sync.dma_start(out=xh_tok[:, 128 * i:128 * (i + 1)],
                                           in_=xn_dram.ap()[i][:, 128 * h:128 * (h + 1)]), "s_xh", 16)

                for i4 in range(4):

                    @block.tensor
                    def _(tensor, i4=i4):
                        wait(tensor, "s_xh")
                        if i4 >= 1:
                            wait(tensor, "da")
                        for ii in range(4):
                            i = 4 * i4 + ii
                            ins = tensor.transpose(psf[4][:, 128 * ii:128 * (ii + 1)],
                                                   xh_tok[:, 128 * i:128 * (i + 1)], idft[:])
                        inc(ins, "dt")

                    @block.scalar
                    def _(scalar, i4=i4):
                        wait(scalar, "dt")
                        inc(scalar.activation(xn_h[:, 512 * i4:512 * (i4 + 1)], psf[4][:], AF.Copy), "da")

                for wsb, bsb, dst in ((igw_sb, gbt, sgx), (agw_sb, abt, sga)):

                    @block.tensor
                    def _(tensor, h=h, wsb=wsb):
                        wait(tensor, "da")
                        for c4 in range(4):
                            ins = tensor.matmul(psf[c4][:], wsb[:, 128 * h:128 * (h + 1)],
                                                xn_h[:, 512 * c4:512 * (c4 + 1)],
                                                start=True, stop=True)
                        inc(ins, "dt")

                    @block.scalar
                    def _(scalar, h=h, bsb=bsb, dst=dst):
                        wait(scalar, "dt")
                        for c4 in range(4):
                            ins = scalar.activation(dst[:, 512 * c4:512 * (c4 + 1)], psf[c4][:],
                                                    AF.Sigmoid, bias=bsb[:, h:h + 1])
                        inc(ins, "da")

                @block.vector
                def _(vector, h=h):
                    wait(vector, "da")
                    inc(vector.tensor_single_scalar(loga[:], sga[:], msp8[:, h:h + 1], ALU.mult), "dv")

                @block.scalar
                def _(scalar):
                    wait(scalar, "dv")
                    inc(scalar.activation(a_t[:], loga[:], AF.Exp), "da")
                    wait(scalar, "da")
                    inc(scalar.activation(asq[:], a_t[:], AF.Square), "da")
                    wait(scalar, "da")
                    inc(scalar.activation(mlt[:], asq[:], AF.Sqrt, bias=1.0, scale=-1.0), "da")

                @block.vector
                def _(vector, h=h):
                    wait(vector, "da")
                    vector.tensor_scalar(mlt[:, 0:1], mlt[:, 0:1], cfa[:, 0:1], cfb[:, 0:1], ALU.mult, ALU.add)
                    inc(vector.tensor_scalar(mlt[:, 1024:1025], mlt[:, 1024:1025],
                                             cfa[:, 1:2], cfb[:, 1:2], ALU.mult, ALU.add), "dv")
                    inc(vector.tensor_tensor(nx1[:], xn_h[:], sgx[:], ALU.mult), "dv")
                    wait(vector, "dv")
                    inc(vector.tensor_tensor(nx[:], nx1[:], mlt[:], ALU.mult), "dv")
                    wait(vector, "dv")
                    if h >= 1:
                        wait(vector, "s_y")
                    inc(vector.tensor_tensor_scan(ysc[:], a_t[:], nx[:], 0.0, ALU.mult, ALU.add), "dv")

                @block.sync
                def _(sync, h=h):
                    wait(sync, "dv")
                    dma4(sync, y_dram.ap()[h], ysc[:, NTOK:E], "s_y", n=2)

            p2_dv = ctrs["dv"].v
            p2_da = ctrs["da"].v
            p2_sy = ctrs["s_y"].v

        with ExitStack() as p4:
            e4 = p4.enter_context
            dec_g = e4(nc.sbuf_tensor("dec_g", [128, 16 * 1024], BF16))
            dec_u = e4(nc.sbuf_tensor("dec_u", [128, 16 * 1024], BF16))
            wd_dec = e4(nc.sbuf_tensor("wd_dec", [128, 8 * 2048], BF16))
            inter = e4(nc.sbuf_tensor("inter", [128, 8 * 1024], BF16))
            i0b = e4(nc.sbuf_tensor("i0b", [128, 8 * 1024], BF16))
            pk_t = [e4(nc.sbuf_tensor(f"pk_t{p}", [128, 1024], U16)) for p in range(2)]
            sc_t = [e4(nc.sbuf_tensor(f"sc_t{p}", [128, 1024], BF16)) for p in range(2)]
            base_b = [[e4(nc.sbuf_tensor(f"base_{r}_{k}", [128, 1024], U16)) for k in range(2)] for r in range(2)]
            t1_b = [e4(nc.sbuf_tensor(f"t1_{r}", [128, 1024], U16)) for r in range(2)]
            tb_b = [e4(nc.sbuf_tensor(f"tb_{k}", [128, 1024], BF16)) for k in range(2)]
            mag_b = [e4(nc.sbuf_tensor(f"mag_{k}", [128, 1024], BF16)) for k in range(2)]
            gf_st = e4(nc.sbuf_tensor("gf_st", [128, 1024], F32))
            q2_st = e4(nc.sbuf_tensor("q2_st", [128, 1024], F32))
            sm_st = e4(nc.sbuf_tensor("sm_st", [128, 1024], F32))
            ub_st = sm_st[:].bitcast(BF16)[:, 0:1024]
            pb = [e4(nc.psum_tensor(f"pb{i}", [128, 512], F32)) for i in range(8)]

            SH2 = {"lo": 6, "hi": 2}
            SH1 = {"lo": 12, "hi": 8}
            load_free = {}
            lctr = [0]
            uctr = [0]
            prev = [None]
            ph2_g1 = {}
            ph2_g2 = {}
            ph2_g3 = {}

            class U:
                __slots__ = ("uidx", "lidx", "slot", "nibs", "dsts", "par",
                             "war_dt", "spk", "ssc", "ph1_dv", "act_da")

            def emit_unit(pk_src, sc_src, nibs, dsts, war_dt):
                u = U()
                u.uidx = uctr[0]; uctr[0] += 1
                u.lidx = lctr[0]; lctr[0] += 1
                u.slot = u.lidx % 2
                u.par = u.uidx % 2
                u.nibs, u.dsts, u.war_dt = nibs, dsts, war_dt
                pkn = f"s_pk{u.slot}"
                scn = f"s_sc{u.slot}"

                @block.sync
                def _(sync):
                    if u.uidx == 0:
                        wait(sync, "s_scl", scl_done)
                        wait(sync, "s_y", p2_sy)
                        wait(sync, "dv", p2_dv)
                        wait(sync, "da", p2_da)
                        wait(sync, "s_xn0")
                        wait(sync, "s_xn1")
                    if u.lidx >= 2:
                        wait(sync, "dv", load_free[u.lidx - 2])
                    inc(sync.dma_start(out=pk_t[u.slot][:], in_=pk_src), pkn, 16)
                    inc(sync.dma_start(out=sc_t[u.slot][:], in_=sc_src), scn, 16)
                u.spk = ctrs[pkn].v
                u.ssc = ctrs[scn].v

                @block.vector
                def _(vector):
                    wait(vector, pkn, u.spk)
                    pk = pk_t[u.slot][:]
                    nk = len(u.nibs)
                    for k, nib in enumerate(u.nibs):
                        vector.tensor_scalar(base_b[u.par][k][:], pk, SH2[nib], 0x1C0,
                                             ALU.logical_shift_left, ALU.bitwise_and)
                    ins = vector.tensor_scalar(t1_b[u.par][:], pk, SH1[u.nibs[0]], 0x8000,
                                               ALU.logical_shift_left, ALU.bitwise_and)
                    inc(ins, "dv")
                    wait(vector, "dv")
                    if nk == 2:
                        vector.tensor_scalar(pk, pk, SH1[u.nibs[1]], 0x8000,
                                             ALU.logical_shift_left, ALU.bitwise_and)
                    for k in range(nk):
                        ins = vector.tensor_scalar(base_b[u.par][k][:], base_b[u.par][k][:],
                                                   0x3F00, None, ALU.add)
                    inc(ins, "dv")
                u.ph1_dv = ctrs["dv"].v

                if prev[0] is not None:
                    emit_ph2(prev[0])
                    prev[0] = None
                prev[0] = u

            def emit_ph2(u):
                @block.vector
                def _(vector):
                    wait(vector, "dv", u.ph1_dv)
                    nk = len(u.nibs)
                    t1s = [t1_b[u.par][:], pk_t[u.slot][:]]
                    for k in range(nk):
                        ins = vector.tensor_scalar(tb_b[k][:], base_b[u.par][k][:].bitcast(BF16),
                                                   2.0, -1.0, ALU.mult, ALU.add)
                    inc(ins, "dv")
                    wait(vector, "dv")
                    for k in range(nk):
                        ins = vector.tensor_tensor(mag_b[k][:], tb_b[k][:],
                                                   base_b[u.par][k][:].bitcast(BF16), ALU.min)
                    inc(ins, "dv")
                    ph2_g1[u.uidx] = ctrs["dv"].v
                    wait(vector, "dv")
                    for k in range(nk):
                        ins = vector.tensor_tensor(base_b[u.par][k][:], mag_b[k][:].bitcast(U16),
                                                   t1s[k], ALU.bitwise_or)
                    inc(ins, "dv")
                    ph2_g2[u.uidx] = ctrs["dv"].v
                    wait(vector, "dv")
                    wait(vector, f"s_sc{u.slot}", u.ssc)
                    if u.war_dt is not None:
                        wait(vector, "dt", u.war_dt)
                    for k in range(nk):
                        ins = vector.tensor_tensor(u.dsts[k], base_b[u.par][k][:].bitcast(BF16),
                                                   sc_t[u.slot][:], ALU.mult)
                    inc(ins, "dv")
                    ph2_g3[u.uidx] = ctrs["dv"].v
                load_free[u.lidx] = ctrs["dv"].v

            def flush():
                if prev[0] is not None:
                    emit_ph2(prev[0])
                    prev[0] = None
                return ctrs["dv"].v

            gelu_rrsa_dv = {}
            gelu_i0_dv = {}
            gelu_bank_dv = {}
            ub_bank_dv = {}

            def emit_gfq2(pg, gate_dt):
                ntl = pg % 4

                @block.vector
                def _(vector):
                    wait(vector, "dt", gate_dt)
                    vector.tensor_copy(gf_st[:, 0:512], pb[2 * ntl][:])
                    ins = vector.tensor_copy(gf_st[:, 512:1024], pb[2 * ntl + 1][:])
                    inc(ins, "ev")
                    gelu_bank_dv[pg] = ctrs["ev"].v
                    wait(vector, "ev")
                    vector.tensor_tensor(q2_st[:, 0:512], gf_st[:, 0:512],
                                         gf_st[:, 0:512], ALU.mult)
                    ins = vector.tensor_tensor(q2_st[:, 512:1024], gf_st[:, 512:1024],
                                               gf_st[:, 512:1024], ALU.mult)
                    inc(ins, "dv")
                    wait(vector, "dv")
                    inc(vector.tensor_scalar(q2_st[:], q2_st[:], GC1, GC2, ALU.mult, ALU.add), "dv")
                    wait(vector, "dv")
                    inc(vector.tensor_tensor(sm_st[:], q2_st[:], gf_st[:], ALU.mult), "dv")
                gelu_rrsa_dv[pg] = ctrs["dv"].v
                return gelu_rrsa_dv[pg]

            def emit_sig(pg):
                @block.scalar
                def _(scalar):
                    wait(scalar, "dv", gelu_rrsa_dv[pg])
                    inc(scalar.activation(q2_st[:], sm_st[:], AF.Sigmoid), "da")
                return ctrs["da"].v

            def emit_i0(pg, sig_da, grp):
                nt = 4 * grp + pg % 4

                @block.vector
                def _(vector):
                    wait(vector, "da", sig_da)
                    wait(vector, "s_acc0")
                    wait(vector, "s_acc1")
                    inc(vector.tensor_tensor(
                        i0b[:, 1024 * nt:1024 * (nt + 1)],
                        q2_st[:], gf_st[:], ALU.mult), "dv")
                gelu_i0_dv[pg] = ctrs["dv"].v

            PENDING = []

            def drain(n=None):
                k = len(PENDING) if n is None else min(n, len(PENDING))
                for _ in range(k):
                    PENDING.pop(0)()

            evac_dv = {}
            dq_holder = {}
            down_dt = {}

            for sg in range(NSG):
                lo_region = sg < 8
                m = sg % 8
                war_g = gate_war_dt if sg > 0 else None
                war_u = up_war_dt if sg > 0 else None

                g_u0 = uctr[0]
                for kbt in range(KBT):
                    emit_unit(pkg.ap()[kbt][:, SGN * sg:SGN * (sg + 1)],
                              sgbf_d.ap()[8 * kbt:8 * (kbt + 1), SGN * sg:SGN * (sg + 1)]
                              .unsqueeze(1).broadcast_to((8, 16, SGN)),
                              ["lo", "hi"],
                              [dec_g[:, 1024 * (2 * kbt):1024 * (2 * kbt + 1)],
                               dec_g[:, 1024 * (2 * kbt + 1):1024 * (2 * kbt + 2)]], war_g)
                    drain(1)
                g_done = flush()
                drain()
                g3s = [ph2_g3[g_u0 + kbt] for kbt in range(KBT)]

                @block.tensor
                def _(tensor, g3s=g3s, sg=sg):
                    if sg > 0:
                        wait(tensor, "ev", evac_dv[(sg - 1, 3)])
                    for step in range(16):
                        kbt, nib = step // 2, step % 2
                        if nib == 0:
                            wait(tensor, "dv", g3s[kbt])
                        xsrc = xnE if nib == 0 else xnO
                        for ntl in range(4):
                            for tc in range(2):
                                ins = tensor.matmul(
                                    pb[2 * ntl + tc][:],
                                    dec_g[:, 1024 * step + 128 * ntl:1024 * step + 128 * (ntl + 1)],
                                    xsrc[:, 1024 * kbt + 512 * tc:1024 * kbt + 512 * (tc + 1)],
                                    start=(step == 0), stop=(step == 15))
                    inc(ins, "dt")
                gate0_dt = ctrs["dt"].v

                def queue_gelu(grp, gate_dt):
                    for j in range(4):
                        pg = 4 * grp + j

                        def piece(pg=pg, grp=grp, gate_dt=gate_dt):
                            emit_gfq2(pg, gate_dt)
                            sd_ = emit_sig(pg)
                            emit_i0(pg, sd_, grp)
                        PENDING.append(piece)

                queue_gelu(0, gate0_dt)

                u_u0 = uctr[0]
                for kbt in range(4):
                    emit_unit(pku.ap()[kbt][:, SGN * sg:SGN * (sg + 1)],
                              subf_d.ap()[8 * kbt:8 * (kbt + 1), SGN * sg:SGN * (sg + 1)]
                              .unsqueeze(1).broadcast_to((8, 16, SGN)),
                              ["lo", "hi"],
                              [dec_u[:, 1024 * (2 * kbt):1024 * (2 * kbt + 1)],
                               dec_u[:, 1024 * (2 * kbt + 1):1024 * (2 * kbt + 2)]], war_u)
                    drain(2)
                drain()

                @block.tensor
                def _(tensor):
                    wait(tensor, "ev", gelu_bank_dv[3])
                    for step in range(16):
                        kbt, nib = step // 2, step % 2
                        xsrc = xnE if nib == 0 else xnO
                        for ntl in range(4):
                            nt = 4 + ntl
                            for tc in range(2):
                                ins = tensor.matmul(
                                    pb[2 * ntl + tc][:],
                                    dec_g[:, 1024 * step + 128 * nt:1024 * step + 128 * (nt + 1)],
                                    xsrc[:, 1024 * kbt + 512 * tc:1024 * kbt + 512 * (tc + 1)],
                                    start=(step == 0), stop=(step == 15))
                    inc(ins, "dt")
                gate1_dt = ctrs["dt"].v
                gate_war_dt = gate1_dt

                queue_gelu(1, gate1_dt)

                for kbt in range(4, KBT):
                    emit_unit(pku.ap()[kbt][:, SGN * sg:SGN * (sg + 1)],
                              subf_d.ap()[8 * kbt:8 * (kbt + 1), SGN * sg:SGN * (sg + 1)]
                              .unsqueeze(1).broadcast_to((8, 16, SGN)),
                              ["lo", "hi"],
                              [dec_u[:, 1024 * (2 * kbt):1024 * (2 * kbt + 1)],
                               dec_u[:, 1024 * (2 * kbt + 1):1024 * (2 * kbt + 2)]], war_u)
                    drain(2)
                u_done = flush()
                drain()
                u_g3s = [ph2_g3[u_u0 + kbt] for kbt in range(KBT)]

                @block.tensor
                def _(tensor, u_g3s=u_g3s):
                    wait(tensor, "ev", gelu_bank_dv[7])
                    for step in range(16):
                        kbt, nib = step // 2, step % 2
                        if nib == 0:
                            wait(tensor, "dv", u_g3s[kbt])
                        xsrc = xnE if nib == 0 else xnO
                        for ntl in range(4):
                            for tc in range(2):
                                ins = tensor.matmul(
                                    pb[2 * ntl + tc][:],
                                    dec_u[:, 1024 * step + 128 * ntl:1024 * step + 128 * (ntl + 1)],
                                    xsrc[:, 1024 * kbt + 512 * tc:1024 * kbt + 512 * (tc + 1)],
                                    start=(step == 0), stop=(step == 15))
                    inc(ins, "dt")
                up0_dt = ctrs["dt"].v

                ubinter_dv = {}

                def queue_ubinter(grp, up_dt):
                    for j in range(4):
                        pj = 4 * grp + j
                        nt = 4 * grp + j

                        def piece(pj=pj, ntl=j, nt=nt, up_dt=up_dt):
                            @block.vector
                            def _(vector):
                                wait(vector, "dt", up_dt)
                                vector.tensor_copy(sm_st[:].bitcast(BF16)[:, 0:512],
                                                   pb[2 * ntl][:])
                                ins = vector.tensor_copy(sm_st[:].bitcast(BF16)[:, 512:1024],
                                                         pb[2 * ntl + 1][:])
                                inc(ins, "ev")
                                ub_bank_dv[pj] = ctrs["ev"].v
                                wait(vector, "ev")
                                inc(vector.tensor_tensor(
                                    inter[:, 1024 * nt:1024 * (nt + 1)],
                                    i0b[:, 1024 * nt:1024 * (nt + 1)],
                                    ub_st, ALU.mult), "dv")
                            ubinter_dv[pj] = ctrs["dv"].v
                        PENDING.append(piece)

                def wd_units(j):
                    nbt = 8 * m + j
                    nib = "lo" if lo_region else "hi"
                    war_d = down_dt.get(sg - 1)
                    for halfload in range(2):
                        base = 2048 * j + 1024 * halfload
                        emit_unit(pkd.ap()[nbt][:, 1024 * halfload:1024 * (halfload + 1)],
                                  sdbf_d.ap()[8 * nbt:8 * (nbt + 1), 1024 * halfload:1024 * (halfload + 1)]
                                  .unsqueeze(1).broadcast_to((8, 16, 1024)),
                                  [nib], [wd_dec[:, base:base + 1024]], war_d)

                queue_ubinter(0, up0_dt)
                for j in range(4):
                    wd_units(j)
                    drain(2)
                drain()

                @block.tensor
                def _(tensor):
                    wait(tensor, "ev", ub_bank_dv[3])
                    for step in range(16):
                        kbt, nib = step // 2, step % 2
                        xsrc = xnE if nib == 0 else xnO
                        for ntl in range(4):
                            nt = 4 + ntl
                            for tc in range(2):
                                ins = tensor.matmul(
                                    pb[2 * ntl + tc][:],
                                    dec_u[:, 1024 * step + 128 * nt:1024 * step + 128 * (nt + 1)],
                                    xsrc[:, 1024 * kbt + 512 * tc:1024 * kbt + 512 * (tc + 1)],
                                    start=(step == 0), stop=(step == 15))
                    inc(ins, "dt")
                up1_dt = ctrs["dt"].v
                up_war_dt = up1_dt

                queue_ubinter(1, up1_dt)
                for j in range(4, 8):
                    wd_units(j)
                    drain(2)
                wd_done = flush()
                drain()
                inter_done = ctrs["dv"].v

                st_acc_q = {}
                for quad in range(4):

                    def quad_mm(quad=quad, sg=sg, wd_done=wd_done, inter_done=inter_done):
                        @block.tensor
                        def _(tensor):
                            wait(tensor, "dv", max(wd_done, inter_done))
                            if quad > 0:
                                wait(tensor, "ev", evac_dv[(sg, quad - 1)])
                            for j in range(8):
                                for htl in range(4):
                                    ht = 4 * quad + htl
                                    for tc in range(2):
                                        ins = tensor.matmul(
                                            pb[2 * htl + tc][:],
                                            wd_dec[:, 2048 * j + 128 * ht:2048 * j + 128 * (ht + 1)],
                                            inter[:, 1024 * j + 512 * tc:1024 * j + 512 * (tc + 1)],
                                            start=(j == 0), stop=(j == 7))
                            inc(ins, "dt")
                        dq_holder[(sg, quad)] = ctrs["dt"].v
                        if quad == 3:
                            down_dt[sg] = ctrs["dt"].v
                    PENDING.append(quad_mm)

                    def quad_evacs(quad=quad, sg=sg, st_acc_q=st_acc_q):
                        dq = dq_holder[(sg, quad)]
                        i0f = i0b[:].bitcast(F32)
                        add_dv = {}
                        for htl in range(4):
                            ht = 4 * quad + htl
                            Ls = i0f[:, 1024 * (htl % 2):1024 * (htl % 2) + 1024]
                            Os = i0f[:, 2048 + 1024 * (htl % 2):2048 + 1024 * (htl % 2) + 1024]
                            ldn = f"s_ld{htl % 2}"
                            acn = f"s_acc{htl % 2}"

                            @block.sync
                            def _(sync, ht=ht, sg=sg, htl=htl, Ls=Ls, ldn=ldn):
                                if sg > 0:
                                    if htl >= 2:
                                        wait(sync, "ev", add_dv[htl - 2])
                                    dma4(sync, Ls, oacc_dram.ap()[ht], ldn, n=2)
                            ld_ = ctrs[ldn].v

                            @block.vector
                            def _(vector, htl=htl, dq=dq, ld_=ld_, sg=sg, quad=quad,
                                  Ls=Ls, Os=Os, ldn=ldn, acn=acn):
                                wait(vector, "dt", dq)
                                if htl >= 2:
                                    wait(vector, acn, st_acc_q[(quad, htl - 2)])
                                elif quad > 0:
                                    wait(vector, acn, st_acc_q[(quad - 1, htl + 2)])
                                if sg > 0:
                                    wait(vector, ldn, ld_)
                                    vector.tensor_tensor(Os[:, 0:512], Ls[:, 0:512],
                                                         pb[2 * htl + 0][:], ALU.add)
                                    inc(vector.tensor_tensor(Os[:, 512:1024], Ls[:, 512:1024],
                                                             pb[2 * htl + 1][:], ALU.add), "ev")
                                else:
                                    vector.tensor_copy(Os[:, 0:512], pb[2 * htl + 0][:])
                                    inc(vector.tensor_copy(Os[:, 512:1024], pb[2 * htl + 1][:]), "ev")
                            add_dv[htl] = ctrs["ev"].v

                            @block.sync
                            def _(sync, ht=ht, htl=htl, Os=Os, acn=acn):
                                wait(sync, "ev", add_dv[htl])
                                dma4(sync, oacc_dram.ap()[ht], Os, acn, n=2)
                            st_acc_q[(quad, htl)] = ctrs[acn].v
                        evac_dv[(sg, quad)] = ctrs["ev"].v
                    PENDING.append(quad_evacs)

            drain()

            p4_acc0 = ctrs["s_acc0"].v
            p4_acc1 = ctrs["s_acc1"].v

            if DBG:
                @block.sync
                def _(sync):
                    wait(sync, "dv")
                    wait(sync, "s_acc0", p4_acc0)
                    wait(sync, "s_acc1", p4_acc1)
                    inc(sync.dma_start(out=dbg_dec.ap(), in_=dec_g[:]), "s_l5", 16)
                    inc(sync.dma_start(out=dbg_wd.ap(), in_=wd_dec[:]), "s_l5", 16)
                    inc(sync.dma_start(out=dbg_int.ap(), in_=inter[:]), "s_l5", 16)
                    inc(sync.dma_start(out=dbg_i0.ap(), in_=i0b[:]), "s_l5", 16)
                    inc(sync.dma_start(out=dbg_xne.ap(), in_=xnE[:]), "s_l5", 16)
                    for ht in range(H):
                        inc(sync.dma_start(out=dbg_oac.ap()[ht], in_=oacc_dram.ap()[ht]), "s_l5", 16)
                    wait(sync, "s_l5")

        with ExitStack() as p5:
            e5 = p5.enter_context
            tot = e5(nc.sbuf_tensor("tot", [128, H * 1024], F32))
            acc_t = e5(nc.sbuf_tensor("acc_t", [128, 1024], F32))
            yh_t = e5(nc.sbuf_tensor("yh_t", [128, 1024], F32))
            xo_t = e5(nc.sbuf_tensor("xo_t", [128, D], F32))
            ot_t = e5(nc.sbuf_tensor("ot_t", [128, D], F32))
            res_t = e5(nc.sbuf_tensor("res_t", [128, D], F32))
            ps5 = e5(nc.psum_tensor("ps5", [128, 512], F32))

            for ht in range(H):

                @block.sync
                def _(sync, ht=ht):
                    if ht == 0:
                        wait(sync, "s_acc0", p4_acc0)
                        wait(sync, "s_acc1", p4_acc1)
                        wait(sync, "s_y", 16 * 2 * H)
                        wait(sync, "dv")
                        wait(sync, "da")
                    else:
                        wait(sync, "dv")
                    dma4(sync, acc_t[:], oacc_dram.ap()[ht], "s_l5", n=2)
                    dma4(sync, yh_t[:], y_dram.ap()[ht], "s_l5", n=2)

                @block.vector
                def _(vector, ht=ht):
                    wait(vector, "s_l5")
                    inc(vector.tensor_tensor(tot[:, 1024 * ht:1024 * (ht + 1)], acc_t[:], yh_t[:], ALU.add), "dv")

            for tt_ in range(8):

                @block.sync
                def _(sync, tt_=tt_):
                    wait(sync, "dv")
                    if tt_ >= 1:
                        wait(sync, "s_out")
                    dma4(sync, xo_t[:], x_ext.ap()[NTOK + 128 * tt_:NTOK + 128 * (tt_ + 1), :], "s_l5")

                for h4 in range(4):

                    @block.tensor
                    def _(tensor, tt_=tt_, h4=h4):
                        wait(tensor, "dv")
                        if not (tt_ == 0 and h4 == 0):
                            wait(tensor, "da")
                        for hh in range(4):
                            hx = 4 * h4 + hh
                            ins = tensor.transpose(ps5[:, 128 * hh:128 * (hh + 1)],
                                                   tot[:, 1024 * hx + 128 * tt_:1024 * hx + 128 * (tt_ + 1)],
                                                   idft[:])
                        inc(ins, "dt")

                    @block.scalar
                    def _(scalar, h4=h4):
                        wait(scalar, "dt")
                        inc(scalar.activation(ot_t[:, 512 * h4:512 * (h4 + 1)], ps5[:], AF.Copy), "da")

                @block.vector
                def _(vector):
                    wait(vector, "da")
                    wait(vector, "s_l5")
                    inc(vector.tensor_tensor(res_t[:], ot_t[:], xo_t[:], ALU.add), "dv")

                @block.sync
                def _(sync, tt_=tt_):
                    wait(sync, "dv")
                    dma4(sync, y_out.ap()[128 * tt_:128 * (tt_ + 1), :], res_t[:], "s_out")

            @block.sync
            def _(sync):
                wait(sync, "s_out")

    return nc


def kernel(L0_attn_x, L0_attn_a_param, L0_attn_ig_w, L0_attn_ig_b,
           L0_attn_ag_w, L0_attn_ag_b, L0_ffn_gate_W_packed, L0_ffn_gate_scales,
           L0_ffn_up_W_packed, L0_ffn_up_scales, L0_ffn_down_W_packed,
           L0_ffn_down_scales):
    X = np.ascontiguousarray(np.asarray(L0_attn_x, dtype=np.float32))

    perm = np.concatenate([np.arange(0, N, 2), np.arange(1, N, 2)])

    def pack_u16(W_packed):
        w = np.ascontiguousarray(np.asarray(W_packed, dtype=np.int32))
        return w.view(np.uint16)[:, 0::2]

    pkg_h = np.ascontiguousarray(pack_u16(L0_ffn_gate_W_packed)[perm].T).reshape(KBT, 128, N)
    pku_h = np.ascontiguousarray(pack_u16(L0_ffn_up_W_packed)[perm].T).reshape(KBT, 128, N)
    pkd_h = np.ascontiguousarray(pack_u16(L0_ffn_down_W_packed).T).reshape(64, 128, D)
    sgT_h = np.ascontiguousarray(np.asarray(L0_ffn_gate_scales, dtype=np.float32)[perm].T)
    suT_h = np.ascontiguousarray(np.asarray(L0_ffn_up_scales, dtype=np.float32)[perm].T)
    sdT_h = np.ascontiguousarray(np.asarray(L0_ffn_down_scales, dtype=np.float32).T)

    igw_h = np.ascontiguousarray(np.asarray(L0_attn_ig_w, dtype=np.float32))
    agw_h = np.ascontiguousarray(np.asarray(L0_attn_ag_w, dtype=np.float32))
    gbT_h = np.ascontiguousarray(np.asarray(L0_attn_ig_b, dtype=np.float32).T)
    abT_h = np.ascontiguousarray(np.asarray(L0_attn_ag_b, dtype=np.float32).T)
    apT_h = np.ascontiguousarray(np.asarray(L0_attn_a_param, dtype=np.float32).reshape(H, 128).T)
    idf_h = np.eye(128, dtype=np.float32)
    idb_h = np.eye(128).astype(ml_dtypes.bfloat16)
    cst_h = np.tile(np.array([[-2.0, -6.0, EPS, -1.0, 1.0, 0.0, 0.0, 0.0]],
                             dtype=np.float32), (128, 1))

    in_maps = []
    for c in range(8):
        b, th = c // 2, c % 2
        own = X[b, NTOK * th:NTOK * (th + 1)]
        pre = X[b, 0:NTOK] if th == 1 else np.zeros((NTOK, D), dtype=np.float32)
        x_ext_h = np.ascontiguousarray(np.concatenate([pre, own], axis=0))
        if th == 1:
            cfA_h = np.concatenate([np.zeros((128, 1)), np.ones((128, 1))], axis=1).astype(np.float32)
            cfB_h = np.concatenate([np.ones((128, 1)), np.zeros((128, 1))], axis=1).astype(np.float32)
        else:
            cfA_h = np.concatenate([np.ones((128, 1)), np.zeros((128, 1))], axis=1).astype(np.float32)
            cfB_h = np.concatenate([np.zeros((128, 1)), np.ones((128, 1))], axis=1).astype(np.float32)
        in_maps.append(dict(
            x_ext=x_ext_h, igw=igw_h, agw=agw_h, gbT=gbT_h, abT=abT_h,
            apT=apT_h, cfA=cfA_h, cfB=cfB_h, idf=idf_h, idb=idb_h, cst=cst_h,
            pkg=pkg_h, pku=pku_h, pkd=pkd_h, sgT=sgT_h, suT=suT_h, sdT=sdT_h))

    nc = build()
    res = run_bass_kernel_spmd(nc, in_maps, core_ids=list(range(8)))

    out = np.empty((B, L, D), dtype=np.float32)
    for c in range(8):
        b, th = c // 2, c % 2
        out[b, NTOK * th:NTOK * (th + 1)] = res.results[c]["y_out"]
    return out
